# version 01 (final)
# baseline (speedup 1.0000x reference)
import numpy as np
import ml_dtypes
import contextlib
import os
import concourse.bass as bass
import concourse.mybir as mybir

F32 = mybir.dt.float32
BF16 = mybir.dt.bfloat16
I32 = mybir.dt.int32
AF = mybir.ActivationFunctionType
OP = mybir.AluOpType
ds = bass.ds
AP = bass.AP
BIG = 1.0e18
LN_EPS = 1e-5
QTH = (0.25, 0.75, 1.25, 1.75, 2.5, 2.5, 3.5, 3.5, 5.0, 5.0, 5.0, 5.0)
VQOPS = 5 + len(QTH) + 3
VLN = 8
SLN = 1


CTREG = {}


def _pool_ids(eng, nc_):
    hs = []
    while True:
        try:
            hs.append(eng.alloc_register(f"pp{len(hs)}_{nc_.next_id()}"))
        except ValueError:
            break
    ids = {nc_.lookup_reg(h).reg_id for h in hs}
    for h in hs:
        eng.free_register(h)
    return ids


def _special_ids(eng, nc_):
    ids = set()

    def add(h):
        if h is None:
            return
        try:
            ids.add(nc_.lookup_reg(h).reg_id)
        except Exception:
            pass

    for nm in ("zero_reg", "sbuf_scratch_reg"):
        try:
            add(getattr(eng, nm))
        except Exception:
            pass
    try:
        p = eng.tpb_base_reg
        if p is not None:
            add(p.lo)
            add(p.hi)
    except Exception:
        pass
    try:
        for p in (eng.bc_regs or []):
            add(p.lo)
            add(p.hi)
    except Exception:
        pass
    try:
        pid = getattr(eng, "_cached_partition_id", None)
        if pid is not None:
            for hh in pid.val.handles:
                add(hh)
    except Exception:
        pass
    return ids


@contextlib.contextmanager
def regscope(eng):
    nc_ = eng.bass
    start_free = _pool_ids(eng, nc_)
    yield
    end_free = _pool_ids(eng, nc_)
    keep = _special_ids(eng, nc_)
    for rid in sorted(start_free - end_free - keep):
        hr = nc_.add_register(eng.engine, f"rsc{rid}_{nc_.next_id()}", rid)
        eng.free_register(hr)


def sect(dec):
    def wrap(fn):
        def inner(eng):
            nc_ = eng.bass
            start_free = _pool_ids(eng, nc_)
            fn(eng)
            for r in CTREG.pop(id(eng), []):
                eng.free_register(r)
            end_free = _pool_ids(eng, nc_)
            keep = _special_ids(eng, nc_)
            for rid in sorted(start_free - end_free - keep):
                hr = nc_.add_register(eng.engine, f"rcl{rid}_{nc_.next_id()}",
                                      rid)
                eng.free_register(hr)
        return dec(inner)
    return wrap


class Ctr:
    def __init__(self, eng, sem, name):
        self.eng = eng
        self.sem = sem
        self.reg = eng.alloc_register(name)
        CTREG.setdefault(id(eng), []).append(self.reg)
        eng.reg_mov(self.reg, 0)

    def wait(self, delta):
        if delta:
            self.eng.reg_add(self.reg, self.reg, delta)
        self.eng.wait_ge(self.sem, self.reg)

    def bump(self, delta):
        if delta:
            self.eng.reg_add(self.reg, self.reg, delta)


KSTOP = int(os.environ.get("KSTOP", "99"))
KP0 = int(os.environ.get("KP0", "2"))
KQG = int(os.environ.get("KQG", "3"))


def finish_stub(nc, out_d, src, OWN, C):
    NTB_ = OWN // 128
    with nc.semaphore("zOs") as zO, nc.Block() as bz:
        @bz.gpsimd
        def _(g):
            for r_ in range(NTB_):
                g.dma_start(out=out_d[r_ * 128:(r_ + 1) * 128, 0:C],
                            in_=src[:, 0:C]).then_inc(zO, 16)
            g.wait_ge(zO, NTB_ * 16)
        @bz.tensor
        def _(t):
            t.wait_ge(zO, NTB_ * 16)


if os.environ.get("KNOFORI"):
    @contextlib.contextmanager
    def fori(eng, start, end):
        assert end - start == 1, "KNOFORI only supports 1-trip loops"
        yield start


def probe_regs(nc, tag):
    if not os.environ.get("KPROBE"):
        return
    for nm, en in (("V", nc.vector), ("S", nc.scalar), ("T", nc.tensor),
                   ("G", nc.gpsimd), ("E", nc.sync)):
        rs = []
        while True:
            try:
                rs.append(en.alloc_register(f"prb{tag}{nm}{len(rs)}"))
            except ValueError:
                break
        for r in rs:
            en.free_register(r)
        print("PROBE", tag, nm, "free:", len(rs))


@contextlib.contextmanager
def fori(eng, start, end):
    nc_ = eng.bass
    name = f"kfori_{nc_.next_id()}"
    ls, le = f"{name}_loop", f"{name}_end"
    engines = [eng.engine]
    regs = nc_.alloc_registers(f"{name}_i", engines=engines)
    nc_.regs_mov(regs, start)
    nc_.br(ls, engines=engines)
    sv = None
    with nc_.body(ls, valid_engines=engines):
        sv = nc_.snap(regs, engines=engines, min_val=start, max_val=end - 1)
        yield sv
        nc_.regs_alu(regs, regs, 1, op=OP.add)
        nc_.br_lt(regs, end, on_true=ls, on_false=le, engines=engines)
    nc_.switch_bb(le)
    for hnd in regs:
        nc_.free_register(hnd)
    try:
        for hnd in sv.val.handles:
            nc_.free_register(hnd)
    except Exception:
        pass


def build_nc(B, T, C, I, NCORE):
    AX = mybir.AxisListType.X
    TOK = B * T
    OWN = TOK // NCORE
    PT = min(512, T)
    NT1 = T // PT
    NSUB = PT // 128
    CG = C // 128
    FD = CG * B
    K = 16
    NGC = (T - 1 + K - 1) // K
    if NGC % 2 == 1:
        NGC += 1
    NSC = NGC * K
    W = K + 2
    TKT = NSC + 2 * K + W + 8
    TEH = NSC + 4
    CH = min(1024, T)
    NCH = T // CH
    NTB = OWN // 128
    NBL = max(1, OWN // 512)
    BL = OWN // NBL
    NSB = BL // 128
    ICG = I // 128
    OTS = []
    o = 0
    while o < I:
        OTS.append((o, min(512, I - o)))
        o += 512

    nc = bass.Bass("TRN2", target_bir_lowering=False)
    x_all = nc.dram_tensor("x_all", [TOK, C], F32, kind="ExternalInput")
    xsl = nc.dram_tensor("xsl", [OWN + 1, C], F32, kind="ExternalInput")
    Wk_d = nc.dram_tensor("Wk", [C, C], F32, kind="ExternalInput")
    Wv_d = nc.dram_tensor("Wv", [C, C], F32, kind="ExternalInput")
    Wr_d = nc.dram_tensor("Wr", [C, C], F32, kind="ExternalInput")
    Wck_d = nc.dram_tensor("Wck", [C, 4 * C], F32, kind="ExternalInput")
    Wcv_d = nc.dram_tensor("Wcv", [4 * C, C], F32, kind="ExternalInput")
    Wcr_d = nc.dram_tensor("Wcr", [C, C], F32, kind="ExternalInput")
    wg_in = nc.dram_tensor("wg_in", [I, C], F32, kind="ExternalInput")
    wu_in = nc.dram_tensor("wu_in", [I, C], F32, kind="ExternalInput")
    wd_in = nc.dram_tensor("wd_in", [C, I], F32, kind="ExternalInput")
    wlan = nc.dram_tensor("wlan", [128, FD], F32, kind="ExternalInput")
    ulan = nc.dram_tensor("ulan", [128, FD], F32, kind="ExternalInput")
    g1r = nc.dram_tensor("g1r", [128, C], F32, kind="ExternalInput")
    b1r = nc.dram_tensor("b1r", [128, C], F32, kind="ExternalInput")
    g2r = nc.dram_tensor("g2r", [128, C], F32, kind="ExternalInput")
    b2r = nc.dram_tensor("b2r", [128, C], F32, kind="ExternalInput")
    tsr = nc.dram_tensor("tsr", [128, C], F32, kind="ExternalInput")
    iden = nc.dram_tensor("iden", [128, 128], BF16, kind="ExternalInput")
    m0_in = nc.dram_tensor("m0_in", [1, 1], F32, kind="ExternalInput")
    eps_in = nc.dram_tensor("eps_in", [128, 1], F32, kind="ExternalInput")
    syo_in = nc.dram_tensor("syo_in", [1, 1], I32, kind="ExternalInput")
    out_d = nc.dram_tensor("out", [OWN, C], F32, kind="ExternalOutput")
    KTd = nc.dram_tensor("KTd", [C, B, TKT], F32, kind="Internal")
    VTd = nc.dram_tensor("VTd", [C, B, T], F32, kind="Internal")
    RTd = nc.dram_tensor("RTd", [C, B, T], F32, kind="Internal")
    EHd = nc.dram_tensor("EHd", [C, B, TEH], F32, kind="Internal")
    SYd = nc.dram_tensor("SYd", [C, TOK + 128], BF16, kind="Internal")
    PARd = nc.dram_tensor("PARd", [OWN, C], F32, kind="Internal")
    WGqT = nc.dram_tensor("WGqT", [C, I], BF16, kind="Internal")
    WUqT = nc.dram_tensor("WUqT", [C, I], BF16, kind="Internal")
    WDqT = nc.dram_tensor("WDqT", [I, C], BF16, kind="Internal")
    WCKb = nc.dram_tensor("WCKb", [C, 4 * C], BF16, kind="Internal")
    WCVb = nc.dram_tensor("WCVb", [4 * C, C], BF16, kind="Internal")
    WCRb = nc.dram_tensor("WCRb", [C, C], BF16, kind="Internal")

    st_top = contextlib.ExitStack()

    def mk(st, fn, *a):
        return st.enter_context(fn(*a))

    wla = mk(st_top, nc.sbuf_tensor, "wla", [128, FD], F32)
    ula = mk(st_top, nc.sbuf_tensor, "ula", [128, FD], F32)
    eul = mk(st_top, nc.sbuf_tensor, "eul", [128, FD], F32)
    g1t = mk(st_top, nc.sbuf_tensor, "g1t", [128, C], F32)
    b1t = mk(st_top, nc.sbuf_tensor, "b1t", [128, C], F32)
    g2t = mk(st_top, nc.sbuf_tensor, "g2t", [128, C], F32)
    b2t = mk(st_top, nc.sbuf_tensor, "b2t", [128, C], F32)
    tst = mk(st_top, nc.sbuf_tensor, "tst", [128, C], F32)
    tsb16 = mk(st_top, nc.sbuf_tensor, "tsb16", [1, C], BF16)
    idn = mk(st_top, nc.sbuf_tensor, "idn", [128, 128], BF16)
    idnf = mk(st_top, nc.sbuf_tensor, "idnf", [128, 128], F32)
    m0t = mk(st_top, nc.sbuf_tensor, "m0t", [1, 1], F32)
    syot = mk(st_top, nc.sbuf_tensor, "syot", [1, 1], I32)
    epst = mk(st_top, nc.sbuf_tensor, "epst", [128, 1], F32)
    red = mk(st_top, nc.sbuf_tensor, "red", [128, 1], F32)
    red2 = mk(st_top, nc.sbuf_tensor, "red2", [128, 1], F32)
    sca = mk(st_top, nc.sbuf_tensor, "sca", [128, 1], F32)
    scb = mk(st_top, nc.sbuf_tensor, "scb", [128, 1], F32)

    def emit_ln_v(v, src_ap, dst_ap, cols, sem_v, sem_s, rv, rvs, sq_ap,
                  p=128):
        rs_ = red[0:p, 0:1]; r2 = red2[0:p, 0:1]
        sa = sca[0:p, 0:1]; sb_ = scb[0:p, 0:1]
        v.tensor_tensor(out=sq_ap, in0=src_ap, in1=src_ap, op=OP.mult
                        ).then_inc(sem_v, 1)
        v.tensor_reduce(out=rs_, in_=src_ap, axis=AX, op=OP.add
                        ).then_inc(sem_v, 1)
        rv.wait(2)
        v.tensor_reduce(out=r2, in_=sq_ap, axis=AX, op=OP.add
                        ).then_inc(sem_v, 1)
        v.tensor_scalar(out=sa, in0=rs_, scalar1=1.0 / cols, scalar2=None,
                        op0=OP.mult).then_inc(sem_v, 1)
        rv.wait(2)
        v.scalar_tensor_tensor(out=sb_, in0=sa, scalar=-1.0, in1=sa,
                               op0=OP.mult, op1=OP.mult).then_inc(sem_v, 1)
        rv.wait(1)
        v.scalar_tensor_tensor(out=sb_, in0=r2, scalar=1.0 / cols, in1=sb_,
                               op0=OP.mult, op1=OP.add).then_inc(sem_v, 1)
        rv.wait(1)
        rvs.wait(1)
        v.reciprocal(out=sb_, in_=sb_).then_inc(sem_v, 1)
        rv.wait(1)
        v.tensor_scalar(out=dst_ap, in0=src_ap, scalar1=sa, scalar2=sb_,
                        op0=OP.subtract, op1=OP.mult).then_inc(sem_v, 1)
        rv.wait(1)

    def emit_ln_s(s, sem_s, rsv, p=128):
        rsv.wait(6)
        s.activation(out=scb[0:p, 0:1], in_=scb[0:p, 0:1], func=AF.Sqrt,
                     bias=epst[0:p, 0:1]).then_inc(sem_s, 1)


    def make_quant(v, qa, qbm, qsc, qhs, qyb, qlv, qsg, qt1, QW):
        def emit_quant(in_ap, out_ap, ncols, sem_v, wt):
            NB = ncols // 16
            v.scalar_tensor_tensor(out=qa[:, 0:ncols], in0=in_ap, scalar=-1.0,
                                   in1=in_ap, op0=OP.mult, op1=OP.max
                                   ).then_inc(sem_v, 1)
            wt(1)
            v.tensor_reduce(out=qbm[:, 0:NB],
                            in_=AP(qa, 0, [[QW, 128], [16, NB], [1, 16]]),
                            axis=AX, op=OP.max).then_inc(sem_v, 1)
            wt(1)
            v.tensor_scalar(out=qsc[:, 0:NB], in0=qbm[:, 0:NB],
                            scalar1=1.0 / 6.0, scalar2=1e-12,
                            op0=OP.mult, op1=OP.max).then_inc(sem_v, 1)
            wt(1)
            v.reciprocal(out=qhs[:, 0:NB], in_=qsc[:, 0:NB]).then_inc(sem_v, 1)
            wt(1)
            v.tensor_tensor(out=AP(qyb, 0, [[QW, 128], [16, NB], [1, 16]]),
                            in0=AP(qa, 0, [[QW, 128], [16, NB], [1, 16]]),
                            in1=AP(qhs, 0, [[QW // 16, 128], [1, NB], [0, 16]]),
                            op=OP.mult).then_inc(sem_v, 1)
            wt(1)
            v.tensor_single_scalar(out=qlv[:, 0:ncols], in_=qyb[:, 0:ncols],
                                   scalar=QTH[0], op=OP.is_gt).then_inc(sem_v, 1)
            for c_ in QTH[1:]:
                wt(1)
                v.scalar_tensor_tensor(out=qlv[:, 0:ncols],
                                       in0=qyb[:, 0:ncols], scalar=c_,
                                       in1=qlv[:, 0:ncols],
                                       op0=OP.is_gt, op1=OP.add
                                       ).then_inc(sem_v, 1)
            wt(1)
            v.tensor_tensor(out=qt1[:, 0:ncols], in0=qlv[:, 0:ncols],
                            in1=qsg[:, 0:ncols], op=OP.mult).then_inc(sem_v, 1)
            v.tensor_scalar(out=qsc[:, 0:NB], in0=qsc[:, 0:NB], scalar1=0.5,
                            scalar2=None, op0=OP.mult).then_inc(sem_v, 1)
            wt(2)
            v.tensor_tensor(out=out_ap,
                            in0=AP(qt1, 0, [[QW, 128], [16, NB], [1, 16]]),
                            in1=AP(qsc, 0, [[QW // 16, 128], [1, NB], [0, 16]]),
                            op=OP.mult).then_inc(sem_v, 1)
            wt(1)
        return emit_quant

    st0 = contextlib.ExitStack()
    WIOC = max(I, 4 * C)
    wio = mk(st0, nc.sbuf_tensor, "wio", [128, WIOC], F32)
    wob = mk(st0, nc.sbuf_tensor, "wob", [128, WIOC], BF16)
    q0a = mk(st0, nc.sbuf_tensor, "q0a", [128, I], F32)
    q0bm = mk(st0, nc.sbuf_tensor, "q0bm", [128, I // 16], F32)
    q0sc = mk(st0, nc.sbuf_tensor, "q0sc", [128, I // 16], F32)
    q0hs = mk(st0, nc.sbuf_tensor, "q0hs", [128, I // 16], F32)
    q0yb = mk(st0, nc.sbuf_tensor, "q0yb", [128, I], BF16)
    q0lv = mk(st0, nc.sbuf_tensor, "q0lv", [128, I], BF16)
    q0sg = mk(st0, nc.sbuf_tensor, "q0sg", [128, I], F32)
    q0t1 = mk(st0, nc.sbuf_tensor, "q0t1", [128, I], F32)
    tpo = mk(st0, nc.sbuf_tensor, "tpo", [128, WIOC // 128, 128], BF16)
    ptq = mk(st0, nc.psum_tensor, "ptq", [128, 22, 128], BF16)
    cL = mk(st0, nc.semaphore, "cL")
    cLw = mk(st0, nc.semaphore, "cLw")
    cV = mk(st0, nc.semaphore, "cV")
    cS = mk(st0, nc.semaphore, "cS")
    cT = mk(st0, nc.semaphore, "cT")
    cG = mk(st0, nc.semaphore, "cG")
    cGc = mk(st0, nc.semaphore, "cGc")
    NCONST = 11
    CAST = [(Wck_d, WCKb, CG, 4 * C), (Wcv_d, WCVb, 4 * CG, C),
            (Wcr_d, WCRb, CG, C)]
    if KP0 < 1 or KP0 == 3:
        CAST = []
    NC_ = sum(r for _, _, r, _ in CAST)
    QNT = [(wg_in, ICG, C, WGqT), (wu_in, ICG, C, WUqT), (wd_in, CG, I, WDqT)]
    if 1 <= KP0 < 2:
        QNT = []
    _kqn = int(os.environ.get("KQN", "3"))
    _kqr = int(os.environ.get("KQR", "0"))
    QNT = QNT[:_kqn]
    if _kqr:
        QNT = [(w_, min(r_, _kqr), c_, d_) for (w_, r_, c_, d_) in QNT]

    with nc.Block() as b0:
        @sect(b0.vector)
        def _(v):
            q0 = make_quant(v, q0a, q0bm, q0sc, q0hs, q0yb, q0lv, q0sg,
                            q0t1, I)
            rl = Ctr(v, cL, "vcl")
            rlw = Ctr(v, cLw, "vclw")
            rg = Ctr(v, cG, "vcg")
            rv = Ctr(v, cV, "vcv")
            rs = Ctr(v, cS, "vcs")
            rl.wait(NCONST * 16)
            for (w_, d_, rows, cols) in CAST:
                with fori(v, 0, rows):
                    rlw.wait(16)
                    v.tensor_copy(wob[:, 0:cols], wio[:, 0:cols]
                                  ).then_inc(cV, 1)
                    rv.wait(1)
                    rg.wait(16)
            rgc = Ctr(v, cGc, "vcgc")
            for (w_, rows, cols, dT) in QNT:
                with fori(v, 0, rows):
                    rlw.wait(16)
                    rs.wait(1)
                    q0(wio[:, 0:cols], wob[:, 0:cols], cols, cV, rv.wait)
                    if KQG >= 2:
                        rgc.wait(cols // 128)

        @sect(b0.scalar)
        def _(s):
            rl = Ctr(s, cL, "scl")
            rlw = Ctr(s, cLw, "sclw")
            rv = Ctr(s, cV, "scv")
            rl.wait(NCONST * 16)
            s.activation(out=eul[:], in_=ula[:], func=AF.Exp)
            s.activation(out=idnf[:], in_=idn[:], func=AF.Identity)
            s.activation(out=tsb16[:], in_=tst[0:1, 0:C], func=AF.Identity)
            rv.wait(NC_)
            rlw.bump(NC_ * 16)
            rt_s = Ctr(s, cT, "sct")
            rg_s = Ctr(s, cG, "scg")
            if NC_ and KQG >= 3:
                rg_s.bump(NC_ * 16 - 16)
            for (w_, rows, cols, dT) in QNT:
                ntr = cols // 128
                with fori(s, 0, rows):
                    rlw.wait(16)
                    s.activation(out=q0sg[:, 0:cols], in_=wio[:, 0:cols],
                                 func=AF.Sign).then_inc(cS, 1)
                    rv.wait(VQOPS)
                    if KQG >= 3:
                        rg_s.wait(16)
                    if KQG >= 2:
                        rt_s.wait(1)
                        for cc in range(ntr):
                            s.activation(out=tpo[:, cc, :],
                                         in_=ptq[:, cc, :],
                                         func=AF.Identity).then_inc(cGc, 1)

        @sect(b0.tensor)
        def _(t):
            rv = Ctr(t, cV, "tcv")
            rg = Ctr(t, cGc, "tcgc")
            rv.wait(NC_)
            for (w_, rows, cols, dT) in QNT:
                if KQG < 2:
                    break
                ntr = cols // 128
                with fori(t, 0, rows):
                    rv.wait(VQOPS)
                    for cc in range(ntr):
                        ins = t.transpose(out=ptq[:, cc, :],
                                          in_=wob[:, cc * 128:(cc + 1) * 128],
                                          identity=idn[:])
                        if cc == ntr - 1:
                            ins.then_inc(cT, 1)
                    rg.wait(ntr)

        @sect(b0.gpsimd)
        def _(g):
            rv = Ctr(g, cV, "gcv")
            rt = Ctr(g, cT, "gct")
            for (w_, d_, rows, cols) in CAST:
                dv = d_.rearrange("(n p) c -> p n c", p=128)
                with fori(g, 0, rows) as r_:
                    rv.wait(1)
                    g.dma_start(out=dv[:, r_, 0:cols], in_=wob[:, 0:cols]
                                ).then_inc(cG, 16)
            rgo = Ctr(g, cG, "gcgo")
            rgc_g = Ctr(g, cGc, "gcgc")
            rgo.bump(NC_ * 16)
            for (w_, rows, cols, dT) in QNT:
                if KQG < 3:
                    break
                ntr = cols // 128
                dtv = dT.rearrange("(n p) i -> p n i", p=128)
                with fori(g, 0, rows) as r_:
                    sl = ds(r_ * 128, 128)
                    rgc_g.wait(ntr)
                    g.dma_start(out=dtv[:, 0:ntr, sl],
                                in_=tpo[:, 0:ntr, :]).then_inc(cG, 16)
                    rgo.wait(16)

        @sect(b0.sync)
        def _(e):
            rv = Ctr(e, cV, "ecv")
            rs = Ctr(e, cS, "ecs")
            for (dst, src) in ((wla[:], wlan[:]), (ula[:], ulan[:]),
                               (g1t[:], g1r[:]), (b1t[:], b1r[:]),
                               (g2t[:], g2r[:]), (b2t[:], b2r[:]),
                               (tst[:], tsr[:]), (idn[:], iden[:]),
                               (m0t[:], m0_in[:]),
                               (syot[:], syo_in[:]),
                               (epst[:], eps_in[:])):
                e.dma_start(out=dst, in_=src).then_inc(cL, 16)
            for (w_, d_, rows, cols) in CAST:
                wv = w_.rearrange("(n p) c -> p n c", p=128)
                with fori(e, 0, rows) as r_:
                    e.dma_start(out=wio[:, 0:cols], in_=wv[:, r_, 0:cols]
                                ).then_inc(cLw, 16)
                    rv.wait(1)
            for (w_, rows, cols, dT) in QNT:
                wv = w_.rearrange("(n p) c -> p n c", p=128)
                with fori(e, 0, rows) as r_:
                    e.dma_start(out=wio[:, 0:cols], in_=wv[:, r_, 0:cols]
                                ).then_inc(cLw, 16)
                    rv.wait(VQOPS)
                    rs.wait(1)
    st0.close()
    probe_regs(nc, "P0")
    if KSTOP <= 0:
        finish_stub(nc, out_d, g1t, OWN, C)
        st_top.close()
        return nc
    st1 = contextlib.ExitStack()
    WKb = mk(st1, nc.sbuf_tensor, "WKb", [128, CG * C], BF16)
    WVb = mk(st1, nc.sbuf_tensor, "WVb", [128, CG * C], BF16)
    WRb = mk(st1, nc.sbuf_tensor, "WRb", [128, CG * C], BF16)
    wldt = mk(st1, nc.sbuf_tensor, "wldt", [128, C], F32)
    xt = mk(st1, nc.sbuf_tensor, "xt", [128, NSUB, C], F32)
    xnt = mk(st1, nc.sbuf_tensor, "xnt", [128, NSUB, C], F32)
    ht = mk(st1, nc.sbuf_tensor, "ht", [128, C], F32)
    sqs = mk(st1, nc.sbuf_tensor, "sqs", [128, C], F32)
    hbt = mk(st1, nc.sbuf_tensor, "hbt", [128, NSUB, C], BF16)
    hst = mk(st1, nc.sbuf_tensor, "hst", [128, NSUB, C], BF16)
    prow = mk(st1, nc.sbuf_tensor, "prow", [1, C], BF16)
    hsT = mk(st1, nc.sbuf_tensor, "hsT", [128, CG, PT], BF16)
    kTt = mk(st1, nc.sbuf_tensor, "kTt", [128, CG, PT], F32)
    vTt = mk(st1, nc.sbuf_tensor, "vTt", [128, CG, PT], F32)
    rTt = mk(st1, nc.sbuf_tensor, "rTt", [128, CG, PT], F32)
    zrp = mk(st1, nc.sbuf_tensor, "zrp", [128, TKT - T], F32)
    pmm = mk(st1, nc.psum_tensor, "pmm", [128, 4, PT], F32)
    ptb = mk(st1, nc.psum_tensor, "ptb", [128, 32, 128], BF16)
    pL = mk(st1, nc.semaphore, "pL")
    xLs = mk(st1, nc.semaphore, "xLs")
    pV = mk(st1, nc.semaphore, "pV")
    pS = mk(st1, nc.semaphore, "pS")
    pX = mk(st1, nc.semaphore, "pX")
    pG = mk(st1, nc.semaphore, "pG")
    pT = mk(st1, nc.semaphore, "pT")
    pE = mk(st1, nc.semaphore, "pE")
    pM = mk(st1, nc.semaphore, "pM")
    pSe = mk(st1, nc.semaphore, "pSe")
    pSp = mk(st1, nc.semaphore, "pSp")

    with nc.Block() as b1:
        @sect(b1.vector)
        def _(v):
            rl = Ctr(v, pL, "vpl")
            rx = Ctr(v, xLs, "vxl")
            rv = Ctr(v, pV, "vpv")
            rvs = Ctr(v, pS, "vps")
            rt = Ctr(v, pT, "vpt")
            rvm = Ctr(v, pM, "vpm")
            rvg = Ctr(v, pG, "vpg")
            for wi, dstW in enumerate((WKb, WVb, WRb)):
                with fori(v, 0, CG) as r_:
                    rl.wait(16)
                    v.tensor_copy(dstW[:, ds(r_ * C, C)], wldt[:, 0:C]
                                  ).then_inc(pX, 1)
            v.memset(zrp[:], 0.0).then_inc(pX, 1)
            for b_ in range(B):
                rvg.bump(16)
                with fori(v, 0, NT1):
                    rx.wait(16)
                    for s_ in range(NSUB):
                        src = xt[:, s_, 0:C]
                        emit_ln_v(v, src, xnt[:, s_, 0:C], C, pV, pS, rv,
                                  rvs, sqs[:, 0:C])
                        emit_ln_v(v, xnt[:, s_, 0:C], ht[:, 0:C], C, pV, pS,
                                  rv, rvs, sqs[:, 0:C])
                        v.scalar_tensor_tensor(out=ht[:, 0:C], in0=ht[:, 0:C],
                                               scalar=1.0, in1=g1t[:],
                                               op0=OP.mult, op1=OP.mult
                                               ).then_inc(pV, 1)
                        rv.wait(1)
                        v.tensor_tensor(out=ht[:, 0:C], in0=ht[:, 0:C],
                                        in1=b1t[:], op=OP.add).then_inc(pV, 1)
                        rv.wait(1)
                        v.tensor_copy(hbt[:, s_, 0:C], ht[:, 0:C]
                                      ).then_inc(pX, 1)
                    rt.wait(1)
                    for tj in range(NSUB * CG):
                        s_, cc = tj // CG, tj % CG
                        v.tensor_copy(hsT[:, cc, s_ * 128:(s_ + 1) * 128],
                                      ptb[:, tj, :]).then_inc(pE, 1)
                    rvg.wait((2 * NSUB + 1) * 16)
                    rvm.wait(3 * CG)

        @sect(b1.scalar)
        def _(s):
            rsv = Ctr(s, pV, "spv")
            rsm = Ctr(s, pM, "spm")
            rssp = Ctr(s, pSp, "ssp")

            def s_tile(waitspills):
                if waitspills:
                    rssp.wait(3 * 16)
                for s_ in range(NSUB):
                    for _ln in range(2):
                        rsv.wait(6)
                        s.activation(out=scb[:, 0:1], in_=scb[:, 0:1],
                                     func=AF.Sqrt,
                                     bias=epst[:, 0:1]).then_inc(pS, 1)
                        rsv.bump(2)
                    rsv.bump(2)
                for mi in range(3 * CG):
                    rsm.wait(1)
                    s.activation(out=(kTt, vTt, rTt)[mi % 3]
                                 [:, mi // 3, 0:PT],
                                 in_=pmm[:, mi % 4, 0:PT],
                                 func=AF.Identity).then_inc(pSe, 1)

            s_tile(False)
            if NT1 > 1:
                with fori(s, 1, NT1):
                    s_tile(True)
            for b_ in range(1, B):
                with fori(s, 0, NT1):
                    s_tile(True)

        @sect(b1.tensor)
        def _(t):
            rg = Ctr(t, pG, "tpg")
            rx = Ctr(t, pX, "tpx")
            re = Ctr(t, pE, "tpe")
            rse = Ctr(t, pSe, "tse")
            rx.wait(3 * CG + 1)
            for b_ in range(B):
                rg.bump(16)
                with fori(t, 0, NT1):
                    rg.wait(NSUB * 2 * 16)
                    for tj in range(NSUB * CG):
                        s_, cc = tj // CG, tj % CG
                        ins = t.transpose(out=ptb[:, tj, :],
                                          in_=hst[:, s_,
                                                  cc * 128:(cc + 1) * 128],
                                          identity=idn[:])
                        if tj == NSUB * CG - 1:
                            ins.then_inc(pT, 1)
                    re.wait(NSUB * CG)
                    rg.bump(16)
                    for mi in range(3 * CG):
                        if mi >= 4:
                            rse.wait(1)
                        wsb = (WKb, WVb, WRb)[mi % 3]
                        co = mi // 3
                        for ci in range(CG):
                            ins = t.matmul(out=pmm[:, mi % 4, 0:PT],
                                           lhsT=wsb[:, ci * C + co * 128:
                                                    ci * C + (co + 1) * 128],
                                           rhs=hsT[:, ci, 0:PT],
                                           start=(ci == 0),
                                           stop=(ci == CG - 1))
                            if ci == CG - 1:
                                ins.then_inc(pM, 1)
                    rse.bump(4)

        @sect(b1.gpsimd)
        def _(g):
            rx = Ctr(g, pX, "gpx")
            rse = Ctr(g, pSe, "gse")
            rgt = Ctr(g, pT, "gpt")
            rgp = Ctr(g, pG, "gpg")
            rx.wait(3 * CG + 1)
            for b_ in range(B):
                g.dma_start(out=prow[:], in_=tsb16[:]).then_inc(pG, 16)
                rgp.wait(16)
                with fori(g, 0, NT1) as i_:
                    for s_ in range(NSUB):
                        rx.wait(1)
                        if s_ == 0:
                            g.dma_start(out=hst[0:1, 0, 0:C], in_=prow[:]
                                        ).then_inc(pG, 16)
                        else:
                            g.dma_start(out=hst[0:1, s_, 0:C],
                                        in_=hbt[127:128, s_ - 1, 0:C]
                                        ).then_inc(pG, 16)
                        g.dma_start(out=hst[1:128, s_, 0:C],
                                    in_=hbt[0:127, s_, 0:C]).then_inc(pG, 16)
                    rgp.wait(2 * NSUB * 16)
                    g.dma_start(out=prow[:], in_=hbt[127:128, NSUB - 1, 0:C]
                                ).then_inc(pG, 16)
                    rgp.wait(16)
                    rgt.wait(1)
                    slT = ds(i_ * PT, PT)
                    rse.wait(3 * CG - 2)
                    for xi, (src_t, dd) in enumerate(((kTt, KTd), (vTt, VTd),
                                                      (rTt, RTd))):
                        if xi:
                            rse.wait(1)
                        dv = dd.rearrange("(n p) b t -> p n b t", p=128)
                        g.dma_start(out=dv[:, 0:CG, b_, slT],
                                    in_=src_t[:, 0:CG, 0:PT]
                                    ).then_inc(pSp, 16)
            for b_ in range(B):
                for co in range(CG):
                    g.dma_start(out=KTd[co * 128:(co + 1) * 128, b_, T:TKT],
                                in_=zrp[:, 0:TKT - T]).then_inc(pSp, 16)
            g.wait_ge(pSp, B * NT1 * 3 * 16 + B * CG * 16)

        @sect(b1.sync)
        def _(e):
            rx = Ctr(e, pX, "epx")
            for idx, srcW in enumerate((Wk_d, Wv_d, Wr_d)):
                wv = srcW.rearrange("(n p) c -> p n c", p=128)
                with fori(e, 0, CG) as r_:
                    e.dma_start(out=wldt[:, 0:C], in_=wv[:, r_, 0:C]
                                ).then_inc(pL, 16)
                    rx.wait(1)
            rx.wait(1)
            xv = x_all.rearrange("(a p) c -> p a c", p=128)
            for b_ in range(B):
                with fori(e, 0, NT1) as i_:
                    e.dma_start(out=xt[:, :, 0:C],
                                in_=xv[:, ds((b_ * NT1 + i_) * NSUB, NSUB),
                                       0:C]).then_inc(xLs, 16)
                    rx.wait(NSUB)
    st1.close()
    probe_regs(nc, "P1")
    if KSTOP <= 1:
        finish_stub(nc, out_d, g1t, OWN, C)
        st_top.close()
        return nc
    st2 = contextlib.ExitStack()
    Cw = mk(st2, nc.sbuf_tensor, "Cw", [128, 2, FD, W + 1], F32)
    Cc = mk(st2, nc.sbuf_tensor, "Cc", [128, 2, W, FD], F32)
    Zc = mk(st2, nc.sbuf_tensor, "Zc", [128, K, FD], F32)
    Est = mk(st2, nc.sbuf_tensor, "Est", [128, 2, FD, K], F32)
    Lcw = mk(st2, nc.sbuf_tensor, "Lcw", [128, K, FD], F32)
    NDc = mk(st2, nc.sbuf_tensor, "NDc", [128, K, FD], F32)
    CZc = mk(st2, nc.sbuf_tensor, "CZc", [128, K, FD], F32)
    Ucw = mk(st2, nc.sbuf_tensor, "Ucw", [128, K, FD], F32)
    P2cb = mk(st2, nc.sbuf_tensor, "P2cb", [128, K, FD], F32)
    ebig = mk(st2, nc.sbuf_tensor, "ebig", [128, FD], F32)
    kL0 = mk(st2, nc.semaphore, "kL0")
    kL1 = mk(st2, nc.semaphore, "kL1")
    hV = mk(st2, nc.semaphore, "hV")
    hP = mk(st2, nc.semaphore, "hP")
    hS = mk(st2, nc.semaphore, "hS")
    hB = mk(st2, nc.semaphore, "hB")
    hEo = mk(st2, nc.semaphore, "hEo")
    hFr = mk(st2, nc.semaphore, "hFr")

    PRE_V = 7
    ktv = KTd.rearrange("(cg p) b t -> p cg b t", p=128)
    ehv = EHd.rearrange("(cg p) b t -> p cg b t", p=128)

    def cwin_ap(buf, b_):
        return AP(Cw, (buf * FD + b_) * (W + 1),
                  [[2 * FD * (W + 1), 128], [B * (W + 1), CG], [1, W + 1]])

    def cw_ap(cb, off):
        return AP(Cw, cb * FD * (W + 1) + off,
                  [[2 * FD * (W + 1), 128], [1, W], [W + 1, FD]])

    with nc.Block() as b2:
        @sect(b2.vector)
        def _(v):
            rv2 = Ctr(v, hV, "rv2")
            rb2 = Ctr(v, hB, "rb2")
            rk0 = Ctr(v, kL0, "rk0")
            rk1 = Ctr(v, kL1, "rk1")
            rk0.wait(32)
            rk1.wait(32)
            wbc = AP(wla, 0, [[FD, 128], [0, W], [1, FD]])
            for cb in range(2):
                v.tensor_sub(Cc[:, cb, :, :], cw_ap(cb, 0),
                             cw_ap(cb, 1)).then_inc(hV, 1)
                rv2.wait(1)
                v.tensor_tensor(out=Cc[:, cb, :, :], in0=Cc[:, cb, :, :],
                                in1=wbc, op=OP.add).then_inc(hP, 1)
            v.tensor_copy(Zc[:, 0, :], Cc[:, 0, 0, :]).then_inc(hV, 1)
            rv2.wait(1)
            v.scalar_tensor_tensor(out=NDc[:, 0, :], in0=Cc[:, 0, 0, :],
                                   scalar=0.0, in1=Cc[:, 0, 1, :],
                                   op0=OP.max, op1=OP.subtract).then_inc(hV, 1)
            rv2.wait(1)
            v.tensor_add(CZc[:, 1, :], Cc[:, 0, 1, :], Zc[:, 0, :]
                         ).then_inc(hV, 1)
            rv2.wait(1)
            v.scalar_tensor_tensor(out=P2cb[:, 0, :], in0=Zc[:, 0, :],
                                   scalar=2.0, in1=NDc[:, 0, :],
                                   op0=OP.mult, op1=OP.subtract).then_inc(hV, 1)
            v.memset(ebig[:], BIG).then_inc(hV, 1)
            rv2.wait(2)
            with fori(v, 0, NGC // 2) as i_:
                for half in range(2):
                    cb = half
                    for j in range(K):
                        rb2.wait(1)
                        v.scalar_tensor_tensor(
                            out=Zc[:, (j + 1) % K, :], in0=Lcw[:, j, :],
                            scalar=2.0, in1=P2cb[:, j, :],
                            op0=OP.mult, op1=OP.add).then_inc(hV, 1)
                        v.tensor_add(Ucw[:, j, :], CZc[:, (j + 1) % K, :],
                                     Lcw[:, j, :]).then_inc(hV, 1)
                        rv2.wait(2)
                        v.tensor_add(CZc[:, (j + 2) % K, :],
                                     Cc[:, cb, j + 2, :],
                                     Zc[:, (j + 1) % K, :]).then_inc(hV, 1)
                        v.scalar_tensor_tensor(
                            out=NDc[:, (j + 1) % K, :], in0=Ucw[:, j, :],
                            scalar=0.0, in1=Cc[:, cb, j + 2, :],
                            op0=OP.max, op1=OP.subtract).then_inc(hV, 1)
                        rv2.wait(2)
                        v.scalar_tensor_tensor(
                            out=P2cb[:, (j + 1) % K, :],
                            in0=Zc[:, (j + 1) % K, :], scalar=2.0,
                            in1=NDc[:, (j + 1) % K, :],
                            op0=OP.mult, op1=OP.subtract).then_inc(hV, 1)
                        rv2.wait(1)
                    (rk0 if cb == 0 else rk1).wait(32)
                    v.tensor_sub(Cc[:, cb, :, :], cw_ap(cb, 0),
                                 cw_ap(cb, 1)).then_inc(hV, 1)
                    rv2.wait(1)
                    v.tensor_tensor(out=Cc[:, cb, :, :], in0=Cc[:, cb, :, :],
                                    in1=wbc, op=OP.add).then_inc(hP, 1)

        @sect(b2.scalar)
        def _(s):
            rsv = Ctr(s, hV, "rsv2")
            rse = Ctr(s, hS, "rse2")
            rsf = Ctr(s, hFr, "rsf2")
            rsv.wait(3)
            with fori(s, 0, NGC // 2) as i_:
                for half in range(2):
                    eb = half
                    rsf.wait(1)
                    for j in range(K):
                        if not (i_ is None):
                            pass
                        s.activation(out=Est[:, eb, :, j], in_=Zc[:, j, :],
                                     func=AF.Exp, scale=-1.0).then_inc(hS, 1)
                        rse.wait(1)
                        s.activation(out=Lcw[:, j, :], in_=Est[:, eb, :, j],
                                     func=AF.Ln, bias=1.0).then_inc(hB, 1)
                        rsv.wait(5)
                    rsv.bump(1)

        @sect(b2.gpsimd)
        def _(g):
            rgv = Ctr(g, hV, "rgv2")
            rgb = Ctr(g, hB, "rgb2")
            rgo = Ctr(g, hEo, "rgo2")
            g.sem_inc(hFr, 2)
            rgv.wait(PRE_V)
            for li in range(FD):
                cg_, b_ = li // B, li % B
                g.dma_start(out=EHd[cg_ * 128:(cg_ + 1) * 128, ds(b_, 1), 0:1],
                            in_=ebig[:, li:li + 1],
                            allow_slow_non_contiguous=True).then_inc(hEo, 16)
            rgo.wait(16 * FD)
            with fori(g, 0, NGC // 2) as i:
                g0 = i * 2
                rgb.wait(K)
                for b_ in range(B):
                    g.dma_start(
                        out=ehv[:, :, ds(b_, 1), ds(g0 * K + 1, K)],
                        in_=AP(Est, b_ * K,
                               [[2 * FD * K, 128], [B * K, CG], [1, K]])
                        ).then_inc(hEo, 16)
                rgo.wait(16 * B)
                g.sem_inc(hFr, 1)
                rgb.wait(K)
                for b_ in range(B):
                    g.dma_start(
                        out=ehv[:, :, ds(b_, 1), ds((g0 + 1) * K + 1, K)],
                        in_=AP(Est, FD * K + b_ * K,
                               [[2 * FD * K, 128], [B * K, CG], [1, K]])
                        ).then_inc(hEo, 16)
                rgo.wait(16 * B)
                g.sem_inc(hFr, 1)

        @sect(b2.sync)
        def _(e):
            rep = Ctr(e, hP, "rep2")
            for b_ in range(B):
                e.dma_start(out=cwin_ap(0, b_),
                            in_=ktv[:, :, ds(b_, 1), 0:W + 1]
                            ).then_inc(kL0, 16)
            for b_ in range(B):
                e.dma_start(out=cwin_ap(1, b_),
                            in_=ktv[:, :, ds(b_, 1), K:K + W + 1]
                            ).then_inc(kL1, 16)
            with fori(e, 1, NGC // 2 + 1) as i:
                stA = ds(i * (2 * K), W + 1)
                stB = ds(i * (2 * K) + K, W + 1)
                rep.wait(1)
                for b_ in range(B):
                    e.dma_start(out=cwin_ap(0, b_),
                                in_=ktv[:, :, ds(b_, 1), stA]
                                ).then_inc(kL0, 16)
                rep.wait(1)
                for b_ in range(B):
                    e.dma_start(out=cwin_ap(1, b_),
                                in_=ktv[:, :, ds(b_, 1), stB]
                                ).then_inc(kL1, 16)
    st2.close()
    probe_regs(nc, "P2")
    if KSTOP <= 2:
        finish_stub(nc, out_d, g1t, OWN, C)
        st_top.close()
        return nc
    st3 = contextlib.ExitStack()
    e3 = mk(st3, nc.sbuf_tensor, "e3", [128, CH], F32)
    v3 = mk(st3, nc.sbuf_tensor, "v3", [128, CH], F32)
    r3 = mk(st3, nc.sbuf_tensor, "r3", [128, CH], F32)
    lam3 = mk(st3, nc.sbuf_tensor, "lam3", [128, CH], F32)
    b3 = mk(st3, nc.sbuf_tensor, "b3", [128, CH], F32)
    rho3 = mk(st3, nc.sbuf_tensor, "rho3", [128, CH], F32)
    rp3 = mk(st3, nc.sbuf_tensor, "rp3", [128, CH], F32)
    mu3 = mk(st3, nc.sbuf_tensor, "mu3", [128, CH], F32)
    sg3 = mk(st3, nc.sbuf_tensor, "sg3", [128, CH], F32)
    y3a = mk(st3, nc.sbuf_tensor, "y3a", [128, CH], F32)
    sy3 = mk(st3, nc.sbuf_tensor, "sy3", [128, CH], BF16)
    car3 = mk(st3, nc.sbuf_tensor, "car3", [128, 1], F32)
    yL = mk(st3, nc.semaphore, "yL")
    yV = mk(st3, nc.semaphore, "yV")
    ySm = mk(st3, nc.semaphore, "ySm")
    yG = mk(st3, nc.semaphore, "yG")
    VY = 14

    with nc.Block() as b3blk:
        @sect(b3blk.vector)
        def _(v):
            rl = Ctr(v, yL, "y_vl")
            rv = Ctr(v, yV, "y_vv")
            rs = Ctr(v, ySm, "y_vs")
            rg = Ctr(v, yG, "y_vg")
            v.memset(sy3[:, 0:CH], 0.0).then_inc(yV, 1)
            rv.bump(1)
            rg.wait(CG * 16)
            for li in range(CG * B):
                cg, b_ = li // B, li % B
                v.memset(car3[:], 0.0).then_inc(yV, 1)
                rv.wait(1)
                with fori(v, 0, NCH):
                    rl.wait(48)
                    v.tensor_scalar(out=lam3[:], in0=e3[:, 0:CH], scalar1=1.0,
                                    scalar2=None, op0=OP.add).then_inc(yV, 1)
                    rv.wait(1)
                    v.reciprocal(out=lam3[:], in_=lam3[:]).then_inc(yV, 1)
                    rv.wait(1)
                    v.tensor_tensor(out=b3[:], in0=lam3[:], in1=v3[:, 0:CH],
                                    op=OP.mult).then_inc(yV, 1)
                    rv.wait(1)
                    v.tensor_sub(b3[:], v3[:, 0:CH], b3[:]).then_inc(yV, 1)
                    rv.wait(1)
                    v.tensor_tensor_scan(out=rho3[:], data0=lam3[:],
                                         data1=b3[:], initial=car3[:, 0:1],
                                         op0=OP.mult, op1=OP.add
                                         ).then_inc(yV, 1)
                    rv.wait(1)
                    v.tensor_copy(rp3[:, 1:CH], rho3[:, 0:CH - 1]
                                  ).then_inc(yV, 1)
                    v.tensor_copy(rp3[:, 0:1], car3[:]).then_inc(yV, 1)
                    rv.wait(2)
                    v.tensor_copy(car3[:], rho3[:, CH - 1:CH]).then_inc(yV, 1)
                    v.tensor_scalar(out=mu3[:], in0=e3[:, 0:CH],
                                    scalar1=eul[:, li:li + 1], scalar2=1.0,
                                    op0=OP.mult, op1=OP.add).then_inc(yV, 1)
                    rv.wait(2)
                    v.reciprocal(out=mu3[:], in_=mu3[:]).then_inc(yV, 1)
                    rv.wait(1)
                    v.tensor_sub(y3a[:], rp3[:], v3[:, 0:CH]).then_inc(yV, 1)
                    rv.wait(1)
                    v.tensor_tensor(out=y3a[:], in0=y3a[:], in1=mu3[:],
                                    op=OP.mult).then_inc(yV, 1)
                    rv.wait(1)
                    v.tensor_add(y3a[:], y3a[:], v3[:, 0:CH]).then_inc(yV, 1)
                    rv.wait(1)
                    rs.wait(1)
                    v.tensor_tensor(out=sy3[:, 0:CH], in0=sg3[:],
                                    in1=y3a[:], op=OP.mult).then_inc(yV, 1)
                    rv.wait(1)
                    rg.wait(16)

        @sect(b3blk.scalar)
        def _(s):
            rl = Ctr(s, yL, "y_sl")
            rv = Ctr(s, yV, "y_sv")
            rv.bump(1)
            for li in range(CG * B):
                rv.wait(1)
                with fori(s, 0, NCH):
                    rl.wait(48)
                    s.activation(out=sg3[:], in_=r3[:, 0:CH], func=AF.Sigmoid
                                 ).then_inc(ySm, 1)
                    rv.wait(VY)

        @sect(b3blk.gpsimd)
        def _(g):
            rv = Ctr(g, yV, "y_gv")
            rv.wait(1)
            for cg_ in range(CG):
                g.dma_start(out=SYd[cg_ * 128:(cg_ + 1) * 128, 0:1],
                            in_=sy3[:, 0:1],
                            allow_slow_non_contiguous=True).then_inc(yG, 16)
            for li in range(CG * B):
                cg, b_ = li // B, li % B
                rv.wait(1)
                with regscope(g):
                    with fori(g, 0, NCH) as ci:
                        rv.wait(VY)
                        g.dma_start(out=SYd[cg * 128:(cg + 1) * 128,
                                            ds(b_ * T + ci * CH + 1, CH)],
                                    in_=sy3[:, 0:CH]).then_inc(yG, 16)

        @sect(b3blk.sync)
        def _(e):
            rv = Ctr(e, yV, "y_ev")
            rv.bump(1)
            first = True
            for li in range(CG * B):
                cg, b_ = li // B, li % B
                sl = slice(cg * 128, (cg + 1) * 128)
                rv.wait(1)
                with regscope(e):
                    with fori(e, 0, NCH) as ci:
                        slC = ds(ci * CH, CH)
                        e.dma_start(out=e3[:, 0:CH],
                                    in_=EHd[sl, b_, slC]).then_inc(yL, 16)
                        e.dma_start(out=v3[:, 0:CH],
                                    in_=VTd[sl, b_, slC]).then_inc(yL, 16)
                        e.dma_start(out=r3[:, 0:CH],
                                    in_=RTd[sl, b_, slC]).then_inc(yL, 16)
                        rv.wait(VY)
                first = False
    st3.close()
    probe_regs(nc, "P3")
    if KSTOP <= 3:
        finish_stub(nc, out_d, g1t, OWN, C)
        st_top.close()
        return nc
    st4 = contextlib.ExitStack()
    x4 = mk(st4, nc.sbuf_tensor, "x4", [128, NSB, C], F32)
    sy4 = mk(st4, nc.sbuf_tensor, "sy4", [128, NSB, C], BF16)
    syf = mk(st4, nc.sbuf_tensor, "syf", [128, C], F32)
    x1t = mk(st4, nc.sbuf_tensor, "x1t", [128, NSB, C], F32)
    hh4 = mk(st4, nc.sbuf_tensor, "hh4", [128, C], F32)
    sq4 = mk(st4, nc.sbuf_tensor, "sq4", [128, C], F32)
    h2bt = mk(st4, nc.sbuf_tensor, "h2bt", [128, NSB, C], BF16)
    h2st = mk(st4, nc.sbuf_tensor, "h2st", [128, NSB, C], BF16)
    rowb = mk(st4, nc.sbuf_tensor, "rowb", [1, C], BF16)
    rsw = mk(st4, nc.sbuf_tensor, "rsw", [1, C], BF16)
    h2sT = mk(st4, nc.sbuf_tensor, "h2sT", [128, CG, BL], BF16)
    wcks = mk(st4, nc.sbuf_tensor, "wcks", [128, CG, 1024], BF16)
    wcvs = mk(st4, nc.sbuf_tensor, "wcvs", [128, C], BF16)
    wcrs = mk(st4, nc.sbuf_tensor, "wcrs", [128, C], BF16)
    kkT = mk(st4, nc.sbuf_tensor, "kkT", [128, 4 * CG, BL], BF16)
    kvT = mk(st4, nc.sbuf_tensor, "kvT", [128, CG, BL], F32)
    rrT = mk(st4, nc.sbuf_tensor, "rrT", [128, CG, BL], F32)
    a2m = mk(st4, nc.sbuf_tensor, "a2m", [128, NSB, C], F32)
    pm4 = mk(st4, nc.psum_tensor, "pm4", [128, 4, 512], F32)
    ptb4 = mk(st4, nc.psum_tensor, "ptb4", [128, 8, 128], BF16)
    ptf4 = mk(st4, nc.psum_tensor, "ptf4", [128, 8, 128], F32)
    aL0 = mk(st4, nc.semaphore, "aL0")
    aLx = mk(st4, nc.semaphore, "aLx")
    aLk = mk(st4, nc.semaphore, "aLk")
    aLw = mk(st4, nc.semaphore, "aLw")
    aV = mk(st4, nc.semaphore, "aV")
    aS = mk(st4, nc.semaphore, "aS")
    aX = mk(st4, nc.semaphore, "aX")
    aG = mk(st4, nc.semaphore, "aG")
    aT = mk(st4, nc.semaphore, "aT")
    aE = mk(st4, nc.semaphore, "aE")
    aK = mk(st4, nc.semaphore, "aK")
    aM = mk(st4, nc.semaphore, "aM")
    aQ = mk(st4, nc.semaphore, "aQ")
    aSe = mk(st4, nc.semaphore, "aSe")
    aSp = mk(st4, nc.semaphore, "aSp")
    NCO = 4 * CG
    if CG * BL >= 3 * C:
        rxw = AP(kvT, 0, [[CG * BL, 1], [1, C]])
        rt1 = AP(kvT, C, [[CG * BL, 1], [1, C]])
        rt2 = AP(kvT, 2 * C, [[CG * BL, 1], [1, C]])
    else:
        rxw = mk(st4, nc.sbuf_tensor, "rxw", [1, C], F32)[0:1, 0:C]
        rt1 = mk(st4, nc.sbuf_tensor, "rt1", [1, C], F32)[0:1, 0:C]
        rt2 = mk(st4, nc.sbuf_tensor, "rt2", [1, C], F32)[0:1, 0:C]

    with nc.Block() as b4a:
        @sect(b4a.vector)
        def _(v):
            rl = Ctr(v, aL0, "a_vl")
            rlx = Ctr(v, aLx, "a_vlx")
            rv = Ctr(v, aV, "a_vv")
            rvs = Ctr(v, aS, "a_vs")
            rt = Ctr(v, aT, "a_vt")
            rvk = Ctr(v, aK, "a_vk")
            rm = Ctr(v, aM, "a_vm")
            rse = Ctr(v, aSe, "a_vse")
            rg = Ctr(v, aG, "a_vg")
            rsp = Ctr(v, aSp, "a_vsp")
            rl.wait(32)
            emit_ln_v(v, rxw, rt1, C, aV, aS, rv, rvs,
                      sq4[0:1, 0:C], p=1)
            v.tensor_copy(rt2, rsw[0:1, 0:C]).then_inc(aV, 1)
            rv.wait(1)
            v.tensor_add(rt1, rt1, rt2
                         ).then_inc(aV, 1)
            rv.wait(1)
            emit_ln_v(v, rt1, rt2, C, aV, aS, rv, rvs,
                      sq4[0:1, 0:C], p=1)
            v.scalar_tensor_tensor(out=rt2, in0=rt2,
                                   scalar=1.0, in1=g2t[0:1, 0:C],
                                   op0=OP.mult, op1=OP.mult).then_inc(aV, 1)
            rv.wait(1)
            v.tensor_tensor(out=rt2, in0=rt2,
                            in1=b2t[0:1, 0:C], op=OP.add).then_inc(aV, 1)
            rv.wait(1)
            v.tensor_sub(rt2, rt2, tst[0:1, 0:C]
                         ).then_inc(aV, 1)
            rv.wait(1)
            v.tensor_scalar(out=rt2, in0=rt2,
                            scalar1=m0t[0:1, 0:1], scalar2=None,
                            op0=OP.mult).then_inc(aV, 1)
            rv.wait(1)
            v.tensor_add(rt2, rt2, tst[0:1, 0:C]
                         ).then_inc(aV, 1)
            rv.wait(1)
            v.tensor_copy(rowb[0:1, 0:C], rt2).then_inc(aX, 1)
            with fori(v, 0, NBL):
                rlx.wait(2 * NSB * 16)
                for s_ in range(NSB):
                    emit_ln_v(v, x4[:, s_, 0:C], x1t[:, s_, 0:C], C, aV, aS,
                              rv, rvs, sq4[:, 0:C])
                    v.tensor_copy(syf[:, 0:C], sy4[:, s_, 0:C]).then_inc(aV, 1)
                    rv.wait(1)
                    v.tensor_add(x1t[:, s_, 0:C], x1t[:, s_, 0:C],
                                 syf[:, 0:C]).then_inc(aV, 1)
                    rv.wait(1)
                    emit_ln_v(v, x1t[:, s_, 0:C], hh4[:, 0:C], C, aV, aS,
                              rv, rvs, sq4[:, 0:C])
                    v.scalar_tensor_tensor(out=hh4[:, 0:C], in0=hh4[:, 0:C],
                                           scalar=1.0, in1=g2t[:],
                                           op0=OP.mult, op1=OP.mult
                                           ).then_inc(aV, 1)
                    rv.wait(1)
                    v.tensor_tensor(out=hh4[:, 0:C], in0=hh4[:, 0:C],
                                    in1=b2t[:], op=OP.add).then_inc(aV, 1)
                    rv.wait(1)
                    v.tensor_copy(h2bt[:, s_, 0:C], hh4[:, 0:C]
                                  ).then_inc(aX, 1)
                for sb in range(NSB * CG // 8):
                    rt.wait(1)
                    for j in range(8):
                        tj = sb * 8 + j
                        s_, cc = tj // CG, tj % CG
                        v.tensor_copy(h2sT[:, cc, s_ * 128:(s_ + 1) * 128],
                                      ptb4[:, j, :]).then_inc(aE, 1)
                for coj in range(NCO):
                    rm.wait(1)
                    v.tensor_scalar(out=sq4[:, 0:BL],
                                    in0=pm4[:, coj % 4, 0:BL], scalar1=0.0,
                                    scalar2=None, op0=OP.max).then_inc(aV, 1)
                    rv.wait(1)
                    v.tensor_tensor(out=kkT[:, coj, 0:BL], in0=sq4[:, 0:BL],
                                    in1=sq4[:, 0:BL], op=OP.mult
                                    ).then_inc(aK, 1)
                    rvk.wait(1)
                rse.wait(2 * CG)
                for co in range(CG):
                    v.tensor_tensor(out=kvT[:, co, 0:BL], in0=kvT[:, co, 0:BL],
                                    in1=rrT[:, co, 0:BL], op=OP.mult
                                    ).then_inc(aV, 1)
                    rv.wait(1)
                for sb in range(NSB * CG // 8):
                    rt.wait(1)
                    for j in range(8):
                        tj = sb * 8 + j
                        cc, s_ = tj // NSB, tj % NSB
                        v.tensor_copy(a2m[:, s_, cc * 128:(cc + 1) * 128],
                                      ptf4[:, j, :]).then_inc(aE, 1)
                for s_ in range(NSB):
                    v.tensor_add(x1t[:, s_, 0:C], x1t[:, s_, 0:C],
                                 x4[:, s_, 0:C]).then_inc(aV, 1)
                    rv.wait(1)
                    v.tensor_add(x1t[:, s_, 0:C], x1t[:, s_, 0:C],
                                 a2m[:, s_, 0:C]).then_inc(aX, 1)
                rsp.wait(NSB * 16)
                rg.wait((2 * NSB + 1) * 16)

        @sect(b4a.scalar)
        def _(s):
            rv = Ctr(s, aV, "a_sv")
            rq = Ctr(s, aQ, "a_sq")
            for li in range(2):
                rv.wait(6)
                s.activation(out=scb[0:1, 0:1], in_=scb[0:1, 0:1],
                             func=AF.Sqrt, bias=epst[0:1, 0:1]).then_inc(aS, 1)
                rv.bump(2)
                if li == 0:
                    rv.bump(2)
            rv.bump(5)
            with fori(s, 0, NBL):
                for s_ in range(NSB):
                    for li in range(2):
                        rv.wait(6)
                        s.activation(out=scb[:, 0:1], in_=scb[:, 0:1],
                                     func=AF.Sqrt,
                                         bias=epst[:, 0:1]
                                     ).then_inc(aS, 1)
                        rv.bump(2)
                        if li == 0:
                            rv.bump(2)
                    rv.bump(2)
                rv.bump(NCO)
                for half in range(2):
                    rq.wait(4 * CG)
                    for cq in range(4):
                        s.activation(out=kvT[:, half * 4 + cq, 0:BL],
                                     in_=pm4[:, cq, 0:BL], func=AF.Identity
                                     ).then_inc(aSe, 1)
                for half in range(2):
                    rq.wait(CG)
                    for cq in range(4):
                        s.activation(out=rrT[:, half * 4 + cq, 0:BL],
                                     in_=pm4[:, cq, 0:BL], func=AF.Sigmoid
                                     ).then_inc(aSe, 1)
                rv.bump(CG + NSB)

        @sect(b4a.tensor)
        def _(t):
            rg = Ctr(t, aG, "a_tg")
            re = Ctr(t, aE, "a_te")
            rk = Ctr(t, aK, "a_tk")
            rlk = Ctr(t, aLk, "a_tlk")
            rlw = Ctr(t, aLw, "a_tlw")
            rse = Ctr(t, aSe, "a_tse")
            rv = Ctr(t, aV, "a_tv")
            rv.bump(2 * VLN + 7)
            with fori(t, 0, NBL):
                rg.wait(NSB * 2 * 16)
                for sb in range(NSB * CG // 8):
                    if sb > 0:
                        re.wait(8)
                    for j in range(8):
                        tj = sb * 8 + j
                        s_, cc = tj // CG, tj % CG
                        ins = t.transpose(out=ptb4[:, j, :],
                                          in_=h2st[:, s_,
                                                  cc * 128:(cc + 1) * 128],
                                          identity=idn[:])
                        if j == 7:
                            ins.then_inc(aT, 1)
                re.wait(8)
                rg.bump(16)
                for co2 in range(4):
                    rlk.wait(CG * 16)
                    for co8 in range(CG):
                        coj = co2 * CG + co8
                        if coj >= 4:
                            rk.wait(1)
                        for ci in range(CG):
                            ins = t.matmul(out=pm4[:, coj % 4, 0:BL],
                                           lhsT=wcks[:, ci,
                                                     co8 * 128:(co8 + 1) * 128],
                                           rhs=h2sT[:, ci, 0:BL],
                                           start=(ci == 0),
                                           stop=(ci == CG - 1))
                            if ci == CG - 1:
                                ins.then_inc(aM, 1)
                rk.wait(4)
                for half in range(2):
                    for ci in range(4 * CG):
                        rlw.wait(16)
                        for cq in range(4):
                            co = half * 4 + cq
                            ins = t.matmul(out=pm4[:, cq, 0:BL],
                                           lhsT=wcvs[:, co * 128:(co + 1) * 128],
                                           rhs=kkT[:, ci, 0:BL],
                                           start=(ci == 0),
                                           stop=(ci == 4 * CG - 1))
                            if cq == 3:
                                ins.then_inc(aQ, 1)
                    rse.wait(4)
                for half in range(2):
                    for ci in range(CG):
                        rlw.wait(16)
                        for cq in range(4):
                            co = half * 4 + cq
                            ins = t.matmul(out=pm4[:, cq, 0:BL],
                                           lhsT=wcrs[:, co * 128:(co + 1) * 128],
                                           rhs=h2sT[:, ci, 0:BL],
                                           start=(ci == 0),
                                           stop=(ci == CG - 1))
                            if cq == 3:
                                ins.then_inc(aQ, 1)
                    rse.wait(4)
                rv.wait(20 * NSB + NCO + CG)
                for sb in range(NSB * CG // 8):
                    if sb > 0:
                        re.wait(8)
                    for j in range(8):
                        tj = sb * 8 + j
                        cc, s_ = tj // NSB, tj % NSB
                        ins = t.transpose(out=ptf4[:, j, :],
                                          in_=kvT[:, cc,
                                                  s_ * 128:(s_ + 1) * 128],
                                          identity=idnf[:])
                        if j == 7:
                            ins.then_inc(aT, 1)
                re.wait(8)
                rv.bump(NSB)

        @sect(b4a.gpsimd)
        def _(g):
            rx = Ctr(g, aX, "a_gx")
            rgg = Ctr(g, aG, "a_gg")
            rx.wait(1)
            with fori(g, 0, NBL) as bl:
                for s_ in range(NSB):
                    rx.wait(1)
                    if s_ == 0:
                        g.dma_start(out=h2st[0:1, 0, 0:C], in_=rowb[:]
                                    ).then_inc(aG, 16)
                    else:
                        g.dma_start(out=h2st[0:1, s_, 0:C],
                                    in_=h2bt[127:128, s_ - 1, 0:C]
                                    ).then_inc(aG, 16)
                    g.dma_start(out=hst[1:128, s_, 0:C] if False else
                                h2st[1:128, s_, 0:C],
                                in_=h2bt[0:127, s_, 0:C]).then_inc(aG, 16)
                rgg.wait(2 * NSB * 16)
                g.dma_start(out=rowb[:], in_=h2bt[127:128, NSB - 1, 0:C]
                            ).then_inc(aG, 16)
                rgg.wait(16)
                for s_ in range(NSB):
                    rx.wait(1)
                    g.dma_start(out=PARd[ds(bl * BL + s_ * 128, 128), 0:C],
                                in_=x1t[:, s_, 0:C]).then_inc(aSp, 16)

        @sect(b4a.sync)
        def _(e):
            rx = Ctr(e, aX, "a_ex")
            rm = Ctr(e, aM, "a_em")
            rq = Ctr(e, aQ, "a_eq")
            rsy = nc.s_assert_within(e.value_load(syot[0:1, 0:1]), 0,
                                     TOK - OWN, skip_runtime_assert=True)
            e.dma_start(out=rxw, in_=xsl[0:1, 0:C]).then_inc(aL0, 16)
            e.dma_start(out=rsw[:],
                        in_=SYd.rearrange("c t -> t c")[ds(rsy, 1), 0:C],
                        allow_slow_non_contiguous=True).then_inc(aL0, 16)
            rx.bump(1)
            with fori(e, 0, NBL) as bl:
                for s_ in range(NSB):
                    e.dma_start(out=x4[:, s_, 0:C],
                                in_=xsl[ds(bl * BL + s_ * 128 + 1, 128), 0:C]
                                ).then_inc(aLx, 16)
                for s_ in range(NSB):
                    e.dma_start_transpose(
                        out=sy4[:, s_, 0:C],
                        in_=SYd[0:C, ds(rsy + bl * BL + s_ * 128 + 1, 128)]
                        ).then_inc(aLx, 16)
                for co2 in range(4):
                    if co2 >= 1:
                        rm.wait(CG)
                    for ci in range(CG):
                        e.dma_start(out=wcks[:, ci, 0:1024],
                                    in_=WCKb[ci * 128:(ci + 1) * 128,
                                             co2 * 1024:(co2 + 1) * 1024]
                                    ).then_inc(aLk, 16)
                for half in range(2):
                    for ci in range(4 * CG):
                        if half > 0 or ci > 0:
                            rq.wait(1)
                        e.dma_start(out=wcvs[:, 0:C],
                                    in_=WCVb[ci * 128:(ci + 1) * 128, 0:C]
                                    ).then_inc(aLw, 16)
                for half in range(2):
                    for ci in range(CG):
                        rq.wait(1)
                        e.dma_start(out=wcrs[:, 0:C],
                                    in_=WCRb[ci * 128:(ci + 1) * 128, 0:C]
                                    ).then_inc(aLw, 16)
                rq.wait(1)
                rm.wait(CG)
                rx.wait(2 * NSB)
    st4.close()
    probe_regs(nc, "P4a")
    if KSTOP <= 4:
        finish_stub(nc, out_d, g1t, OWN, C)
        st_top.close()
        return nc
    HQd = nc.dram_tensor("HQd", [OWN, I], BF16, kind="Internal")
    st5 = contextlib.ExitStack()
    wgq_t = mk(st5, nc.sbuf_tensor, "wgq_t", [128, CG, I], BF16)
    wuq_t = mk(st5, nc.sbuf_tensor, "wuq_t", [128, CG, I], BF16)
    xc5 = mk(st5, nc.sbuf_tensor, "xc5", [128, C], F32)
    xnc5 = mk(st5, nc.sbuf_tensor, "xnc5", [128, C], F32)
    sq5 = mk(st5, nc.sbuf_tensor, "sq5", [128, C], F32)
    xqc5 = mk(st5, nc.sbuf_tensor, "xqc5", [128, C], BF16)
    xqT5 = mk(st5, nc.sbuf_tensor, "xqT5", [128, CG, 128], BF16)
    sgb5 = mk(st5, nc.sbuf_tensor, "sgb5", [128, 512], F32)
    hfc5 = mk(st5, nc.sbuf_tensor, "hfc5", [128, I], F32)
    hqc5 = mk(st5, nc.sbuf_tensor, "hqc5", [128, I], BF16)
    q5a = mk(st5, nc.sbuf_tensor, "q5a", [128, I], BF16)
    q5bm = mk(st5, nc.sbuf_tensor, "q5bm", [128, I // 16], F32)
    q5sc = mk(st5, nc.sbuf_tensor, "q5sc", [128, I // 16], F32)
    q5hs = mk(st5, nc.sbuf_tensor, "q5hs", [128, I // 16], F32)
    q5yb = mk(st5, nc.sbuf_tensor, "q5yb", [128, I], BF16)
    q5lv = mk(st5, nc.sbuf_tensor, "q5lv", [128, I], BF16)
    q5sg = mk(st5, nc.sbuf_tensor, "q5sg", [128, I], BF16)
    q5t1 = mk(st5, nc.sbuf_tensor, "q5t1", [128, I], BF16)
    pm5 = mk(st5, nc.psum_tensor, "pm5", [128, 2, 512], F32)
    ptb5 = mk(st5, nc.psum_tensor, "ptb5", [128, 8, 128], BF16)
    fL = mk(st5, nc.semaphore, "fL")
    fLw = mk(st5, nc.semaphore, "fLw")
    fV = mk(st5, nc.semaphore, "fV")
    fS = mk(st5, nc.semaphore, "fS")
    fSe = mk(st5, nc.semaphore, "fSe")
    fT = mk(st5, nc.semaphore, "fT")
    fE = mk(st5, nc.semaphore, "fE")
    fM = mk(st5, nc.semaphore, "fM")
    fG = mk(st5, nc.semaphore, "fG")

    def q5(v, in_ap, out_ap, ncols, sem_v, wt):
        NB = ncols // 16
        v.scalar_tensor_tensor(out=q5a[:, 0:ncols], in0=in_ap, scalar=-1.0,
                               in1=in_ap, op0=OP.mult, op1=OP.max
                               ).then_inc(sem_v, 1)
        wt(1)
        v.tensor_reduce(out=q5bm[:, 0:NB],
                        in_=AP(q5a, 0, [[I, 128], [16, NB], [1, 16]]),
                        axis=AX, op=OP.max).then_inc(sem_v, 1)
        wt(1)
        v.tensor_scalar(out=q5sc[:, 0:NB], in0=q5bm[:, 0:NB],
                        scalar1=1.0 / 6.0, scalar2=1e-12,
                        op0=OP.mult, op1=OP.max).then_inc(sem_v, 1)
        wt(1)
        v.reciprocal(out=q5hs[:, 0:NB], in_=q5sc[:, 0:NB]).then_inc(sem_v, 1)
        wt(1)
        v.tensor_tensor(out=AP(q5yb, 0, [[I, 128], [16, NB], [1, 16]]),
                        in0=AP(q5a, 0, [[I, 128], [16, NB], [1, 16]]),
                        in1=AP(q5hs, 0, [[I // 16, 128], [1, NB], [0, 16]]),
                        op=OP.mult).then_inc(sem_v, 1)
        wt(1)
        v.tensor_single_scalar(out=q5lv[:, 0:ncols], in_=q5yb[:, 0:ncols],
                               scalar=QTH[0], op=OP.is_gt).then_inc(sem_v, 1)
        for c_ in QTH[1:]:
            wt(1)
            v.scalar_tensor_tensor(out=q5lv[:, 0:ncols], in0=q5yb[:, 0:ncols],
                                   scalar=c_, in1=q5lv[:, 0:ncols],
                                   op0=OP.is_gt, op1=OP.add).then_inc(sem_v, 1)
        wt(1)
        v.tensor_tensor(out=q5t1[:, 0:ncols], in0=q5lv[:, 0:ncols],
                        in1=q5sg[:, 0:ncols], op=OP.mult).then_inc(sem_v, 1)
        v.tensor_scalar(out=q5sc[:, 0:NB], in0=q5sc[:, 0:NB], scalar1=0.5,
                        scalar2=None, op0=OP.mult).then_inc(sem_v, 1)
        wt(2)
        v.tensor_tensor(out=out_ap,
                        in0=AP(q5t1, 0, [[I, 128], [16, NB], [1, 16]]),
                        in1=AP(q5sc, 0, [[I // 16, 128], [1, NB], [0, 16]]),
                        op=OP.mult).then_inc(sem_v, 1)
        wt(1)

    with nc.Block() as b5:
        @sect(b5.vector)
        def _(v):
            rl = Ctr(v, fL, "f_vl")
            rlw = Ctr(v, fLw, "f_vlw")
            rv = Ctr(v, fV, "f_vv")
            rvs = Ctr(v, fS, "f_vs")
            rt = Ctr(v, fT, "f_vt")
            rm = Ctr(v, fM, "f_vm")
            rsev = Ctr(v, fSe, "f_vse")
            rg = Ctr(v, fG, "f_vg")
            rlw.wait(2 * CG * 16)
            with fori(v, 0, NTB):
                rl.wait(16)
                emit_ln_v(v, xc5[:, 0:C], xnc5[:, 0:C], C, fV, fS, rv, rvs,
                          sq5[:, 0:C])
                rvs.wait(1)
                q5(v, xnc5[:, 0:C], xqc5[:, 0:C], C, fV, rv.wait)
                rt.wait(1)
                for cc in range(CG):
                    v.tensor_copy(xqT5[:, cc, 0:128], ptb5[:, cc, :]
                                  ).then_inc(fE, 1)
                for oi, (o0, ow) in enumerate(OTS):
                    rm.wait(2)
                    rsev.wait(1)
                    v.tensor_tensor(out=hfc5[:, o0:o0 + ow],
                                    in0=sgb5[:, 0:ow], in1=pm5[:, 1, 0:ow],
                                    op=OP.mult).then_inc(fV, 1)
                    rv.wait(1)
                rvs.wait(1)
                q5(v, hfc5[:, 0:I], hqc5[:, 0:I], I, fV, rv.wait)
                rg.wait(16)

        @sect(b5.scalar)
        def _(s):
            rv = Ctr(s, fV, "f_sv")
            rm = Ctr(s, fM, "f_sm")
            with fori(s, 0, NTB):
                rv.wait(6)
                s.activation(out=scb[:, 0:1], in_=scb[:, 0:1], func=AF.Sqrt,
                             bias=epst[:, 0:1]).then_inc(fS, 1)
                rv.wait(2)
                s.activation(out=q5sg[:, 0:C], in_=xnc5[:, 0:C], func=AF.Sign
                             ).then_inc(fS, 1)
                rv.bump(VQOPS)
                for oi, (o0, ow) in enumerate(OTS):
                    rm.wait(2)
                    s.activation(out=sgb5[:, 0:ow], in_=pm5[:, 0, 0:ow],
                                 func=AF.Silu).then_inc(fSe, 1)
                    rv.bump(1)
                rv.wait(0)
                s.activation(out=q5sg[:, 0:I], in_=hfc5[:, 0:I], func=AF.Sign
                             ).then_inc(fS, 1)
                rv.bump(VQOPS)

        @sect(b5.tensor)
        def _(t):
            rv = Ctr(t, fV, "f_tv")
            re = Ctr(t, fE, "f_te")
            rse = Ctr(t, fSe, "f_tse")
            with fori(t, 0, NTB):
                rv.wait(8 + VQOPS)
                for cc in range(CG):
                    ins = t.transpose(out=ptb5[:, cc, :],
                                      in_=xqc5[:, cc * 128:(cc + 1) * 128],
                                      identity=idn[:])
                    if cc == CG - 1:
                        ins.then_inc(fT, 1)
                re.wait(CG)
                for oi, (o0, ow) in enumerate(OTS):
                    if oi >= 1:
                        rse.wait(1)
                        rv.wait(1)
                    for ci in range(CG):
                        ins = t.matmul(out=pm5[:, 0, 0:ow],
                                       lhsT=xqT5[:, ci, 0:128],
                                       rhs=wgq_t[:, ci, o0:o0 + ow],
                                       start=(ci == 0), stop=(ci == CG - 1))
                        if ci == CG - 1:
                            ins.then_inc(fM, 1)
                    for ci in range(CG):
                        ins = t.matmul(out=pm5[:, 1, 0:ow],
                                       lhsT=xqT5[:, ci, 0:128],
                                       rhs=wuq_t[:, ci, o0:o0 + ow],
                                       start=(ci == 0), stop=(ci == CG - 1))
                        if ci == CG - 1:
                            ins.then_inc(fM, 1)
                rse.wait(1)
                rv.wait(1 + VQOPS)

        @sect(b5.gpsimd)
        def _(g):
            rv = Ctr(g, fV, "f_gv")
            with fori(g, 0, NTB) as tc:
                rv.wait(8 + VQOPS + len(OTS) + VQOPS)
                g.dma_start(out=HQd[ds(tc * 128, 128), 0:I],
                            in_=hqc5[:, 0:I]).then_inc(fG, 16)

        @sect(b5.sync)
        def _(e):
            rv = Ctr(e, fV, "f_ev")
            for ci in range(CG):
                e.dma_start(out=wgq_t[:, ci, 0:I],
                            in_=WGqT[ci * 128:(ci + 1) * 128, 0:I]
                            ).then_inc(fLw, 16)
            for ci in range(CG):
                e.dma_start(out=wuq_t[:, ci, 0:I],
                            in_=WUqT[ci * 128:(ci + 1) * 128, 0:I]
                            ).then_inc(fLw, 16)
            with fori(e, 0, NTB) as tc:
                e.dma_start(out=xc5[:, 0:C],
                            in_=xsl[ds(tc * 128 + 1, 128), 0:C]
                            ).then_inc(fL, 16)
                rv.wait(8 + VQOPS + len(OTS) + VQOPS)
    st5.close()

    st6 = contextlib.ExitStack()
    wdq_t = mk(st6, nc.sbuf_tensor, "wdq_t", [128, ICG, C], BF16)
    hqc6 = mk(st6, nc.sbuf_tensor, "hqc6", [128, I], BF16)
    hqT6 = mk(st6, nc.sbuf_tensor, "hqT6", [128, ICG, 128], BF16)
    parc6 = mk(st6, nc.sbuf_tensor, "parc6", [128, C], F32)
    ffc6 = mk(st6, nc.sbuf_tensor, "ffc6", [128, C], F32)
    outc6 = mk(st6, nc.sbuf_tensor, "outc6", [128, C], F32)
    pm6 = mk(st6, nc.psum_tensor, "pm6", [128, 2, 512], F32)
    ptb6 = mk(st6, nc.psum_tensor, "ptb6", [128, 22, 128], BF16)
    dL = mk(st6, nc.semaphore, "dL")
    dLw = mk(st6, nc.semaphore, "dLw")
    dV = mk(st6, nc.semaphore, "dV")
    dSe = mk(st6, nc.semaphore, "dSe")
    dT = mk(st6, nc.semaphore, "dT")
    dE = mk(st6, nc.semaphore, "dE")
    dM = mk(st6, nc.semaphore, "dM")
    dOU = mk(st6, nc.semaphore, "dOU")

    with nc.Block() as b6:
        @sect(b6.vector)
        def _(v):
            rl = Ctr(v, dL, "d_vl")
            rlw = Ctr(v, dLw, "d_vlw")
            rt = Ctr(v, dT, "d_vt")
            rse = Ctr(v, dSe, "d_vse")
            rv = Ctr(v, dV, "d_vv")
            ro = Ctr(v, dOU, "d_vo")
            rlw.wait(ICG * 16)
            with fori(v, 0, NTB):
                rl.wait(32)
                rt.wait(1)
                for cc in range(ICG):
                    v.tensor_copy(hqT6[:, cc, 0:128], ptb6[:, cc, :]
                                  ).then_inc(dE, 1)
                rse.wait(2)
                v.tensor_add(outc6[:, 0:C], parc6[:, 0:C], ffc6[:, 0:C]
                             ).then_inc(dV, 1)
                rv.wait(1)
                ro.wait(16)

        @sect(b6.scalar)
        def _(s):
            rm = Ctr(s, dM, "d_sm")
            with fori(s, 0, NTB):
                for co2 in range(2):
                    rm.wait(1)
                    s.activation(out=ffc6[:, co2 * 512:(co2 + 1) * 512],
                                 in_=pm6[:, co2, 0:512], func=AF.Identity
                                 ).then_inc(dSe, 1)

        @sect(b6.tensor)
        def _(t):
            rl = Ctr(t, dL, "d_tl")
            re = Ctr(t, dE, "d_te")
            rse6 = Ctr(t, dSe, "d_tse")
            with fori(t, 0, NTB):
                rl.wait(32)
                for cc in range(ICG):
                    ins = t.transpose(out=ptb6[:, cc, :],
                                      in_=hqc6[:, cc * 128:(cc + 1) * 128],
                                      identity=idn[:])
                    if cc == ICG - 1:
                        ins.then_inc(dT, 1)
                re.wait(ICG)
                for co2 in range(2):
                    for ci in range(ICG):
                        ins = t.matmul(out=pm6[:, co2, 0:512],
                                       lhsT=hqT6[:, ci, 0:128],
                                       rhs=wdq_t[:, ci,
                                                 co2 * 512:(co2 + 1) * 512],
                                       start=(ci == 0), stop=(ci == ICG - 1))
                        if ci == ICG - 1:
                            ins.then_inc(dM, 1)
                rse6.wait(2)

        @sect(b6.gpsimd)
        def _(g):
            rv = Ctr(g, dV, "d_gv")
            with fori(g, 0, NTB) as tc:
                rv.wait(1)
                g.dma_start(out=out_d[ds(tc * 128, 128), 0:C],
                            in_=outc6[:, 0:C]).then_inc(dOU, 16)
            g.wait_ge(dOU, NTB * 16)

        @sect(b6.sync)
        def _(e):
            rv = Ctr(e, dV, "d_ev")
            for ci in range(ICG):
                e.dma_start(out=wdq_t[:, ci, 0:C],
                            in_=WDqT[ci * 128:(ci + 1) * 128, 0:C]
                            ).then_inc(dLw, 16)
            with fori(e, 0, NTB) as tc:
                e.dma_start(out=hqc6[:, 0:I],
                            in_=HQd[ds(tc * 128, 128), 0:I]).then_inc(dL, 16)
                e.dma_start(out=parc6[:, 0:C],
                            in_=PARd[ds(tc * 128, 128), 0:C]).then_inc(dL, 16)
                rv.wait(1)
    st6.close()
    st_top.close()
    return nc

_NC_CACHE = {}


def get_nc(B, T, C, I, NCORE):
    key = (B, T, C, I, NCORE)
    if key not in _NC_CACHE:
        _NC_CACHE[key] = build_nc(B, T, C, I, NCORE)
    return _NC_CACHE[key]


def kernel(L0_attn_x, L0_attn_time_decay, L0_attn_time_first,
           L0_attn_time_shift, L0_attn_Wk, L0_attn_Wv, L0_attn_Wr,
           L0_attn_Wck, L0_attn_Wcv, L0_attn_Wcr, L0_attn_ln1_g,
           L0_attn_ln1_b, L0_attn_ln2_g, L0_attn_ln2_b, L0_ffn_w_gate,
           L0_ffn_w_up, L0_ffn_w_down):
    from concourse.bass_utils import run_bass_kernel_spmd
    x = np.ascontiguousarray(np.asarray(L0_attn_x, dtype=np.float32))
    B, T, C = x.shape
    I = int(np.asarray(L0_ffn_w_gate).shape[0])
    NCORE = 8
    TOK = B * T
    OWN = TOK // NCORE
    CG = C // 128
    FD = CG * B
    xf = x.reshape(TOK, C)

    w = np.ascontiguousarray(np.asarray(L0_attn_time_decay, np.float32))
    u = np.ascontiguousarray(np.asarray(L0_attn_time_first, np.float32))
    wl = np.repeat(w.reshape(CG, 128).T[:, :, None], B, axis=2).reshape(128, FD)
    ul = np.repeat(u.reshape(CG, 128).T[:, :, None], B, axis=2).reshape(128, FD)
    rep = lambda a: np.ascontiguousarray(
        np.broadcast_to(np.asarray(a, np.float32).reshape(1, C), (128, C)))
    ident = np.eye(128, dtype=ml_dtypes.bfloat16)

    nc = get_nc(B, T, C, I, NCORE)
    in_maps = []
    for core in range(NCORE):
        own0 = core * OWN
        lo = max(own0 - 1, 0)
        xs = np.ascontiguousarray(
            np.concatenate([xf[lo:lo + 1], xf[own0:own0 + OWN]], axis=0))
        m0 = 0.0 if own0 % T == 0 else 1.0
        in_maps.append({
            "x_all": xf,
            "xsl": xs,
            "Wk": np.ascontiguousarray(np.asarray(L0_attn_Wk, np.float32)),
            "Wv": np.ascontiguousarray(np.asarray(L0_attn_Wv, np.float32)),
            "Wr": np.ascontiguousarray(np.asarray(L0_attn_Wr, np.float32)),
            "Wck": np.ascontiguousarray(np.asarray(L0_attn_Wck, np.float32)),
            "Wcv": np.ascontiguousarray(np.asarray(L0_attn_Wcv, np.float32)),
            "Wcr": np.ascontiguousarray(np.asarray(L0_attn_Wcr, np.float32)),
            "wg_in": np.ascontiguousarray(np.asarray(L0_ffn_w_gate, np.float32)),
            "wu_in": np.ascontiguousarray(np.asarray(L0_ffn_w_up, np.float32)),
            "wd_in": np.ascontiguousarray(np.asarray(L0_ffn_w_down, np.float32)),
            "wlan": wl,
            "ulan": ul,
            "g1r": rep(L0_attn_ln1_g),
            "b1r": rep(L0_attn_ln1_b),
            "g2r": rep(L0_attn_ln2_g),
            "b2r": rep(L0_attn_ln2_b),
            "tsr": rep(np.asarray(L0_attn_time_shift, np.float32).reshape(C)),
            "iden": ident,
            "m0_in": np.full((1, 1), m0, np.float32),
            "eps_in": np.full((128, 1), 1e-5, np.float32),
            "syo_in": np.full((1, 1), own0, np.int32),
        })
    res = run_bass_kernel_spmd(nc, in_maps, core_ids=list(range(NCORE)))
    out = np.concatenate([np.asarray(r["out"]) for r in res.results], axis=0)
    return out.reshape(B, T, C)
